# version 26
# baseline (speedup 1.0000x reference)
import numpy as np
from contextlib import ExitStack
import concourse.bass as bass
import concourse.mybir as mybir
from concourse.bass_utils import run_bass_kernel_spmd

F32 = mybir.dt.float32
BF16 = mybir.dt.bfloat16
AF = mybir.ActivationFunctionType
ALU = mybir.AluOpType
AX = mybir.AxisListType

D = 4096
TOK = 2048
C = 512
NCH = TOK // C
KT = D // 128
OC = D // 128
NCORES = 8
EPS = 1e-5
SCALING = 2.0


def build_nc():
    nc = bass.Bass("TRN2", target_bir_lowering=False)

    xT = nc.dram_tensor("xT", [D, TOK], F32, kind="ExternalInput").ap()
    wA = nc.dram_tensor("wA", [OC, 128, D + 128], F32, kind="ExternalInput").ap()
    wF = nc.dram_tensor("wF", [OC, 128, D + 128], F32, kind="ExternalInput").ap()
    aT = nc.dram_tensor("aT", [128, D], F32, kind="ExternalInput").ap()
    bT = nc.dram_tensor("bT", [128, D], F32, kind="ExternalInput").ap()
    rT = nc.dram_tensor("rT", [128, 256], F32, kind="ExternalInput").ap()
    idm = nc.dram_tensor("idm", [128, 128], F32, kind="ExternalInput").ap()
    exm = nc.dram_tensor("exm", [8, 128], F32, kind="ExternalInput").ap()
    outd = nc.dram_tensor("out", [D, TOK], F32, kind="ExternalOutput").ap()
    wsc = nc.dram_tensor("wsc", [2, OC, 128, D + 128], BF16, kind="Internal").ap()

    ops = {"T": [], "V": [], "S": [], "G": [], "Y": []}
    cnt = {"t": 0, "v": 0, "s": 0, "x": 0, "w": 0, "o": 0, "g": 0, "gc": 0, "wsc": 0}

    def emit(e, fn):
        ops[e].append(fn)

    def wait(e, sem, val):
        if val > 0:
            emit(e, lambda eng, sem=sem, val=val: eng.wait_ge(sem, val))

    with ExitStack() as _es:
        xt = _es.enter_context(nc.sbuf_tensor("xt", [128, KT * C], F32))
        u = _es.enter_context(nc.sbuf_tensor("u", [128, KT * C], BF16))
        zb = _es.enter_context(nc.sbuf_tensor("zb", [128, OC * C], BF16))
        wb0 = _es.enter_context(nc.sbuf_tensor("wb0", [128, D + 128], BF16))
        wb1 = _es.enter_context(nc.sbuf_tensor("wb1", [128, D + 128], BF16))
        aTs = _es.enter_context(nc.sbuf_tensor("aTs", [128, D], BF16))
        bTs = _es.enter_context(nc.sbuf_tensor("bTs", [128, D], BF16))
        rTs = _es.enter_context(nc.sbuf_tensor("rTs", [128, 256], BF16))
        xbf0 = _es.enter_context(nc.sbuf_tensor("xbf0", [128, C], BF16))
        xbf1 = _es.enter_context(nc.sbuf_tensor("xbf1", [128, C], BF16))
        xbf2 = _es.enter_context(nc.sbuf_tensor("xbf2", [128, C], BF16))
        xbf3 = _es.enter_context(nc.sbuf_tensor("xbf3", [128, C], BF16))
        sq0 = _es.enter_context(nc.sbuf_tensor("sq0", [128, C], BF16))
        sq1 = _es.enter_context(nc.sbuf_tensor("sq1", [128, C], BF16))
        sq2 = _es.enter_context(nc.sbuf_tensor("sq2", [128, C], BF16))
        sq3 = _es.enter_context(nc.sbuf_tensor("sq3", [128, C], BF16))
        tbf = _es.enter_context(nc.sbuf_tensor("tbf", [128, C], BF16))
        ybf0 = _es.enter_context(nc.sbuf_tensor("ybf0", [128, C], BF16))
        ybf1 = _es.enter_context(nc.sbuf_tensor("ybf1", [128, C], BF16))
        mbc_bf = _es.enter_context(nc.sbuf_tensor("mbc_bf", [128, C], BF16))
        ibc_bf = _es.enter_context(nc.sbuf_tensor("ibc_bf", [128, C], BF16))
        ost0 = _es.enter_context(nc.sbuf_tensor("ost0", [128, C], F32))
        ost1 = _es.enter_context(nc.sbuf_tensor("ost1", [128, C], F32))
        row_m = _es.enter_context(nc.sbuf_tensor("row_m", [1, C], F32))
        row_inv = _es.enter_context(nc.sbuf_tensor("row_inv", [1, C], F32))
        row_zm = _es.enter_context(nc.sbuf_tensor("row_zm", [1, C], F32))
        row_a = _es.enter_context(nc.sbuf_tensor("row_a", [1, C], F32))
        row_b = _es.enter_context(nc.sbuf_tensor("row_b", [1, C], F32))
        ones128 = _es.enter_context(nc.sbuf_tensor("ones128", [128, 1], BF16))
        onescol = _es.enter_context(nc.sbuf_tensor("onescol", [1, 128], F32))
        onestile = _es.enter_context(nc.sbuf_tensor("onestile", [128, C], BF16))
        ident = _es.enter_context(nc.sbuf_tensor("ident", [128, 128], BF16))
        expmat = _es.enter_context(nc.sbuf_tensor("expmat", [8, 128], BF16))
        lg = _es.enter_context(nc.sbuf_tensor("lg", [128, 32], F32))
        el = _es.enter_context(nc.sbuf_tensor("el", [128, 32], F32))
        mskt = _es.enter_context(nc.sbuf_tensor("mskt", [128, 32], F32))
        l2t = _es.enter_context(nc.sbuf_tensor("l2t", [128, 8], F32))
        wqt = _es.enter_context(nc.sbuf_tensor("wqt", [128, 32], F32))
        w_f = _es.enter_context(nc.sbuf_tensor("w_f", [128, 32], F32))
        w_bf = _es.enter_context(nc.sbuf_tensor("w_bf", [128, 32], BF16))
        m1s = _es.enter_context(nc.sbuf_tensor("m1s", [128, 4], F32))
        nm1s = _es.enter_context(nc.sbuf_tensor("nm1s", [128, 4], F32))
        m2s = _es.enter_context(nc.sbuf_tensor("m2s", [128, 4], F32))
        e2s = _es.enter_context(nc.sbuf_tensor("e2s", [128, 4], F32))
        dens = _es.enter_context(nc.sbuf_tensor("dens", [128, 4], F32))
        wT_sb = _es.enter_context(nc.sbuf_tensor("wT_sb", [8, C], BF16))
        wer_sb = _es.enter_context(nc.sbuf_tensor("wer_sb", [128, C], BF16))
        hp_bf = _es.enter_context(nc.sbuf_tensor("hp_bf", [128, C], BF16))
        P0 = _es.enter_context(nc.psum_tensor("P0", [128, C], F32))
        P1 = _es.enter_context(nc.psum_tensor("P1", [128, C], F32))
        ps_sq = _es.enter_context(nc.psum_tensor("ps_sq", [1, C], F32))
        ps_sqq = _es.enter_context(nc.psum_tensor("ps_sqq", [1, C], F32))
        ps_mbc = _es.enter_context(nc.psum_tensor("ps_mbc", [128, C], F32))
        ps_ibc = _es.enter_context(nc.psum_tensor("ps_ibc", [128, C], F32))
        ps_h = _es.enter_context(nc.psum_tensor("ps_h", [128, C], F32))
        ps_mix = _es.enter_context(nc.psum_tensor("ps_mix", [128, C], F32))
        s_t = _es.enter_context(nc.semaphore("s_t"))
        s_v = _es.enter_context(nc.semaphore("s_v"))
        s_s = _es.enter_context(nc.semaphore("s_s"))
        s_x = _es.enter_context(nc.semaphore("s_x"))
        s_w = _es.enter_context(nc.semaphore("s_w"))
        s_o = _es.enter_context(nc.semaphore("s_o"))
        s_g = _es.enter_context(nc.semaphore("s_g"))
        s_gc = _es.enter_context(nc.semaphore("s_gc"))
        s_wsc = _es.enter_context(nc.semaphore("s_wsc"))
        P = [P0, P1]
        wb = [wb0, wb1]
        xbf = [xbf0, xbf1, xbf2, xbf3]
        sq = [sq0, sq1, sq2, sq3]
        ost = [ost0, ost1]
        wdram = [wA, wF]

        def t_inc(fn):
            cnt["t"] += 1
            emit("T", lambda t, fn=fn: fn(t).then_inc(s_t, 1))
            return cnt["t"]

        def t_run(fn):
            emit("T", fn)

        def v_inc(fn):
            cnt["v"] += 1
            emit("V", lambda v, fn=fn: fn(v).then_inc(s_v, 1))
            return cnt["v"]

        def s_inc(fn):
            cnt["s"] += 1
            emit("S", lambda s, fn=fn: fn(s).then_inc(s_s, 1))
            return cnt["s"]

        def g_inc(fn):
            cnt["gc"] += 1
            emit("G", lambda g, fn=fn: fn(g).then_inc(s_gc, 1))
            return cnt["gc"]

        emit("G", lambda g: g.dma_start(aTs[:], aT).then_inc(s_g, 16))
        emit("G", lambda g: g.dma_start(bTs[:], bT).then_inc(s_g, 16))
        emit("G", lambda g: g.dma_start(rTs[:], rT).then_inc(s_g, 16))
        cnt["g"] = 3 * 16
        emit("G", lambda g: g.dma_start(ident[:], idm).then_inc(s_g, 16))
        emit("G", lambda g: g.dma_start(expmat[:], exm).then_inc(s_g, 16))
        cnt["g"] += 2 * 16
        g_setup = cnt["g"]

        v_setup0 = v_inc(lambda v: v.memset(ones128[:], 1.0))
        v_setup1 = v_inc(lambda v: v.memset(onescol[:], 1.0))
        v_setup3 = v_inc(lambda v: v.memset(onestile[:], 1.0))
        v_setup = cnt["v"]

        tv_group_end = {}
        vv_f6_done = 0
        vv_u_done = 0
        vv_apply_bank_free = 0
        vv_stats_read = 0
        vv_lg_done = 0
        vv_wer_used = 0
        vv_h_used = 0
        sv_zb_free = 0

        nout = 0

        def ln_rows(ps_need_t, with_zm):
            tgt = row_zm if with_zm else row_m
            wait("V", s_t, ps_need_t)
            v_inc(lambda v, tgt=tgt: v.tensor_scalar_mul(tgt[:], ps_sq[0:1, :], 1.0 / D))
            v_inc(lambda v: v.tensor_scalar_mul(row_a[:], ps_sqq[0:1, :], 1.0 / D))
            v_inc(lambda v, tgt=tgt: v.tensor_mul(row_b[:], tgt[:], tgt[:]))
            v_inc(lambda v: v.tensor_sub(row_a[:], row_a[:], row_b[:]))
            va = v_inc(lambda v: v.tensor_scalar_add(row_a[:], row_a[:], EPS))
            wait("S", s_v, va)
            s_inc(lambda s: s.activation(row_b[:], row_a[:], AF.Ln))
            sa = s_inc(lambda s: s.activation(row_inv[:], row_b[:], AF.Exp, scale=-0.5))
            return cnt["v"], sa

        def bc_pair(m_row, v_need, s_need, t_free_need, s_free_need=0):
            wait("T", s_v, max(v_need, t_free_need))
            wait("T", s_s, s_need)
            ta = t_inc(lambda t, m_row=m_row: t.matmul(ps_mbc[:], onescol[:], m_row[:],
                                                       start=True, stop=True))
            tb = t_inc(lambda t: t.matmul(ps_ibc[:], onescol[:], row_inv[:],
                                          start=True, stop=True))
            wait("S", s_t, tb)
            if s_free_need:
                wait("S", s_v, s_free_need)
            s_inc(lambda s: s.copy(mbc_bf[:], ps_mbc[:]))
            sb = s_inc(lambda s: s.copy(ibc_bf[:], ps_ibc[:]))
            return sb

        f6k = {}
        for ch in range(NCH):
            x_base = cnt["x"]
            t_pair = {}
            s_pair = {}
            for k in range(KT):
                if k % 8 == 0:
                    g = k // 8
                    if ch > 0:
                        wait("S", s_gc, f6k[8 * g + 7])
                    def xload(se, ch=ch, g=g):
                        src = xT[g * 1024:(g + 1) * 1024, ch * C:(ch + 1) * C] \
                            .rearrange("(k p) c -> p k c", p=128)
                        dst = xt[:, g * 8 * C:(g + 1) * 8 * C] \
                            .rearrange("p (k c) -> p k c", c=C)
                        return se.dma_start(dst, src).then_inc(s_x, 16)
                    emit("S", xload)
                    cnt["x"] += 16
                wait("S", s_x, x_base + 16 * (k // 8 + 1))
                if k >= 4:
                    wait("S", s_t, t_pair[k - 4])
                s_inc(lambda s, k=k: s.copy(xbf[k % 4][:], xt[:, k * C:(k + 1) * C]))
                s_pair[k] = s_inc(lambda s, k=k: s.activation(sq[k % 4][:],
                                                              xt[:, k * C:(k + 1) * C],
                                                              AF.Square))
                if k % 2 == 1:
                    if k == 1:
                        wait("T", s_v, max(v_setup, vv_stats_read))
                    wait("T", s_s, s_pair[k])
                    for j in range(k - 1, k + 1):
                        t_run(lambda t, j=j: t.matmul(ps_sq[0:1, :], ones128[:, 0:1],
                                                      xbf[j % 4][:], start=(j == 0),
                                                      stop=(j == KT - 1)))
                    t_run(lambda t, k=k: t.matmul(ps_sqq[0:1, :], ones128[:, 0:1],
                                                  sq[(k - 1) % 4][:], start=(k == 1),
                                                  stop=False))
                    cap = t_inc(lambda t, k=k: t.matmul(ps_sqq[0:1, :], ones128[:, 0:1],
                                                        sq[k % 4][:], start=False,
                                                        stop=(k == KT - 1)))
                    t_pair[k - 1] = cap
                    t_pair[k] = cap
            stats_a_done = t_pair[KT - 1]

            va, sa = ln_rows(stats_a_done, with_zm=False)
            vv_stats_read = va
            bc_a = bc_pair(row_m, va, sa, vv_apply_bank_free, s_free_need=vv_f6_done)

            wait("V", s_s, bc_a)
            wait("V", s_x, cnt["x"])
            if ch > 0:
                wait("V", s_gc, f6k[KT - 1])
            uk = {}
            yb = [ybf0, ybf1]
            for k in range(KT):
                if k >= 2:
                    wait("V", s_gc, uk[k - 2])
                vs = v_inc(lambda v, k=k: v.tensor_sub(yb[k % 2][:],
                                                       xt[:, k * C:(k + 1) * C],
                                                       mbc_bf[:]))
                wait("G", s_v, vs)
                uk[k] = g_inc(lambda g, k=k: g.tensor_mul(u[:, k * C:(k + 1) * C],
                                                          yb[k % 2][:], ibc_bf[:]))
            vv_u_done = cnt["v"]
            u_gc_done = cnt["gc"]
            vv_apply_bank_free = cnt["v"]

            def emit_wload(sidx):
                mat = (sidx % 64) // 32
                oc = sidx % 32
                if sidx >= 2:
                    wait("G", s_t, tv_group_end.get(sidx - 2, 0))
                if sidx < 64:
                    if sidx >= 2:
                        wait("G", s_wsc, 16 * (sidx - 1))
                    emit("G", lambda g, mat=mat, oc=oc, sidx=sidx: g.dma_start(
                        wb[sidx % 2][:], wdram[mat][oc]).then_inc(s_w, 16))
                    cnt["w"] += 16
                    wait("Y", s_w, cnt["w"])
                    emit("Y", lambda y, mat=mat, oc=oc, sidx=sidx: y.dma_start(
                        wsc[mat, oc], wb[sidx % 2][:]).then_inc(s_wsc, 16))
                    cnt["wsc"] += 16
                else:
                    if sidx == 64:
                        wait("G", s_wsc, 16 * 64)
                    emit("G", lambda g, mat=mat, oc=oc, sidx=sidx: g.dma_start(
                        wb[sidx % 2][:], wsc[mat, oc]).then_inc(s_w, 16))
                    cnt["w"] += 16

            def emit_zstats_batch(po_hi):
                wait("T", s_s, ev[po_hi])
                if po_hi == 3:
                    wait("T", s_v, vv_stats_read)
                for j in range(po_hi - 3, po_hi + 1):
                    t_run(lambda t, j=j: t.matmul(ps_sq[0:1, :], ones128[:, 0:1],
                                                  zb[:, j * C:(j + 1) * C],
                                                  start=(j == 0), stop=(j == OC - 1)))
                for j in range(po_hi - 3, po_hi):
                    t_run(lambda t, j=j: t.matmul(ps_sqq[0:1, :], ones128[:, 0:1],
                                                  sq[j % 4][:], start=(j == 0), stop=False))
                cap = t_inc(lambda t, po_hi=po_hi: t.matmul(ps_sqq[0:1, :], ones128[:, 0:1],
                                                            sq[po_hi % 4][:], start=False,
                                                            stop=(po_hi == OC - 1)))
                for j in range(po_hi - 3, po_hi + 1):
                    t_pair[j] = cap

            def emit_evac(po, base):
                wait("S", s_t, tv_group_end[base + po])
                if po >= 4:
                    wait("S", s_t, t_pair[po - 4])
                if base % 64 == 0 and ch > 0 and po == 0:
                    wait("S", s_v, sv_zb_free)
                s_inc(lambda s, po=po: s.copy(zb[:, po * C:(po + 1) * C], P[po % 2][:]))
                ev[po] = s_inc(lambda s, po=po: s.activation(sq[po % 4][:], P[po % 2][:],
                                                             AF.Square))

            ev = {}
            for oc in range(OC):
                sidx = ch * 64 + oc
                emit_wload(sidx)
                wait("T", s_w, 16 * (sidx + 1))
                if oc == 0:
                    wait("T", s_g, g_setup)
                if oc >= 2:
                    wait("T", s_s, ev[oc - 2])
                for k in range(KT):
                    if oc == 0:
                        wait("T", s_gc, uk[k])
                    t_run(lambda t, oc=oc, k=k: t.matmul(P[oc % 2][:],
                                                         wb[oc % 2][:, k * 128:(k + 1) * 128],
                                                         u[:, k * C:(k + 1) * C],
                                                         start=(k == 0), stop=False))
                tv_group_end[sidx] = t_inc(
                    lambda t, oc=oc: t.matmul(P[oc % 2][:],
                                              wb[oc % 2][:, KT * 128:(KT + 1) * 128],
                                              onestile[:], start=False, stop=True))
                if oc >= 1:
                    emit_evac(oc - 1, ch * 64)
                    if (oc - 1) % 4 == 3:
                        emit_zstats_batch(oc - 1)
            emit_evac(OC - 1, ch * 64)
            emit_zstats_batch(OC - 1)
            stats_b_done = t_pair[OC - 1]

            va, sa = ln_rows(stats_b_done, with_zm=True)
            vv_stats_read = va
            wait("S", s_gc, u_gc_done)
            bc_c = bc_pair(row_zm, va, sa, vv_apply_bank_free, s_free_need=vv_u_done)
            wait("V", s_s, bc_c)
            wait("V", s_o, cnt["o"])
            wait("V", s_gc, u_gc_done)
            c3k = {}
            for k in range(KT):
                if k >= 2:
                    wait("V", s_gc, c3k[k - 2])
                v_inc(lambda v, k=k: v.tensor_sub(yb[k % 2][:],
                                                  zb[:, k * C:(k + 1) * C], mbc_bf[:]))
                vc = v_inc(lambda v, k=k: v.tensor_mul(yb[k % 2][:], yb[k % 2][:],
                                                       ibc_bf[:]))
                wait("G", s_v, vc)
                c3k[k] = g_inc(lambda g, k=k: g.tensor_add(
                    xt[:, k * C:(k + 1) * C], xt[:, k * C:(k + 1) * C], yb[k % 2][:]))
            vv_c_done = cnt["v"]
            vv_apply_bank_free = cnt["v"]

            for k in range(KT):
                wait("S", s_gc, c3k[k])
                if k >= 4:
                    wait("S", s_t, t_pair[k - 4])
                s_pair[k] = s_inc(lambda s, k=k: s.activation(sq[k % 4][:],
                                                              xt[:, k * C:(k + 1) * C],
                                                              AF.Square))
                if k % 2 == 1:
                    if k == 1:
                        wait("T", s_v, vv_stats_read)
                    wait("T", s_s, s_pair[k])
                    t_run(lambda t, k=k: t.matmul(ps_sqq[0:1, :], ones128[:, 0:1],
                                                  sq[(k - 1) % 4][:], start=(k == 1),
                                                  stop=False))
                    cap = t_inc(lambda t, k=k: t.matmul(ps_sqq[0:1, :], ones128[:, 0:1],
                                                        sq[k % 4][:], start=False,
                                                        stop=(k == KT - 1)))
                    t_pair[k - 1] = cap
                    t_pair[k] = cap
            stats_d_done = t_pair[KT - 1]

            wait("V", s_t, stats_d_done)
            v_inc(lambda v: v.tensor_scalar_mul(row_a[:], ps_sqq[0:1, :], 1.0 / D))
            v_inc(lambda v: v.tensor_mul(row_b[:], row_m[:], row_m[:]))
            v_inc(lambda v: v.tensor_sub(row_a[:], row_a[:], row_b[:]))
            va = v_inc(lambda v: v.tensor_scalar_add(row_a[:], row_a[:], EPS))
            vv_stats_read = va
            wait("S", s_v, va)
            s_inc(lambda s: s.activation(row_b[:], row_a[:], AF.Ln))
            sa = s_inc(lambda s: s.activation(row_inv[:], row_b[:], AF.Exp, scale=-0.5))
            bc_d = bc_pair(row_m, va, sa, vv_apply_bank_free, s_free_need=vv_c_done)

            wait("V", s_s, bc_d)
            wait("V", s_gc, c3k[KT - 1])
            for k in range(KT):
                if k >= 2:
                    wait("V", s_gc, uk[k - 2])
                vs = v_inc(lambda v, k=k: v.tensor_sub(yb[k % 2][:],
                                                       xt[:, k * C:(k + 1) * C],
                                                       mbc_bf[:]))
                wait("G", s_v, vs)
                uk[k] = g_inc(lambda g, k=k: g.tensor_mul(u[:, k * C:(k + 1) * C],
                                                          yb[k % 2][:], ibc_bf[:]))
            vv_u_done = cnt["v"]
            u_gc_done = cnt["gc"]
            vv_apply_bank_free = cnt["v"]

            wait("T", s_v, vv_wer_used)
            for tt in range(4):
                for k in range(KT):
                    if tt == 0:
                        wait("T", s_gc, uk[k])
                    t_run(lambda t, tt=tt, k=k: t.matmul(
                        ps_mix[:, tt * 8:(tt + 1) * 8],
                        u[:, k * C + tt * 128:k * C + (tt + 1) * 128],
                        rTs[:, k * 8:(k + 1) * 8],
                        start=(k == 0), stop=(k == KT - 1)))
            wait("T", s_v, vv_h_used)
            logits_done = t_inc(lambda t: t.matmul(ps_h[:], aTs[:, 0:128], u[:, 0:C],
                                                   start=True, stop=False))
            for k in range(1, KT):
                t_run(lambda t, k=k: t.matmul(ps_h[:], aTs[:, k * 128:(k + 1) * 128],
                                              u[:, k * C:(k + 1) * C],
                                              start=False, stop=(k == KT - 1)))

            wait("V", s_t, logits_done)
            vv_lg_done = v_inc(lambda v: v.tensor_copy(lg[:], ps_mix[:, 0:32]))
            for tt in range(4):
                r1 = v_inc(lambda v, tt=tt: v.reduce_max(m1s[:, tt:tt + 1],
                                                         lg[:, tt * 8:(tt + 1) * 8],
                                                         axis=AX.X))
                wait("V", s_v, r1)
                v_inc(lambda v, tt=tt: v.tensor_scalar_mul(nm1s[:, tt:tt + 1],
                                                           m1s[:, tt:tt + 1], -1.0))
                rq = v_inc(lambda v, tt=tt: v.tensor_scalar(mskt[:, tt * 8:(tt + 1) * 8],
                                                            lg[:, tt * 8:(tt + 1) * 8],
                                                            m1s[:, tt:tt + 1], None,
                                                            ALU.is_ge))
                wait("V", s_v, rq)
                rl = v_inc(lambda v, tt=tt: v.scalar_tensor_tensor(l2t[:],
                                                                   mskt[:, tt * 8:(tt + 1) * 8],
                                                                   -1e30,
                                                                   lg[:, tt * 8:(tt + 1) * 8],
                                                                   ALU.mult, ALU.add))
                wait("V", s_v, rl)
                r2 = v_inc(lambda v, tt=tt: v.reduce_max(m2s[:, tt:tt + 1], l2t[:],
                                                         axis=AX.X))
                wait("V", s_v, r2)
                v_inc(lambda v, tt=tt: v.tensor_scalar(mskt[:, tt * 8:(tt + 1) * 8],
                                                       lg[:, tt * 8:(tt + 1) * 8],
                                                       m2s[:, tt:tt + 1], None, ALU.is_ge))
            v_rt1 = cnt["v"]
            wait("S", s_v, v_rt1)
            for tt in range(4):
                s_inc(lambda s, tt=tt: s.activation(el[:, tt * 8:(tt + 1) * 8],
                                                    lg[:, tt * 8:(tt + 1) * 8],
                                                    AF.Exp, bias=nm1s[:, tt:tt + 1]))
                s_inc(lambda s, tt=tt: s.activation(e2s[:, tt:tt + 1], m2s[:, tt:tt + 1],
                                                    AF.Exp, bias=nm1s[:, tt:tt + 1]))
            s_rt = cnt["s"]
            wait("V", s_s, s_rt)
            ra = v_inc(lambda v: v.tensor_scalar_add(dens[:], e2s[:], 1.0))
            wait("V", s_v, ra)
            rb = v_inc(lambda v: v.reciprocal(dens[:], dens[:]))
            wait("V", s_v, rb)
            rc = v_inc(lambda v: v.tensor_scalar_mul(dens[:], dens[:], SCALING))
            wait("V", s_v, rc)
            v_inc(lambda v: v.tensor_mul(wqt[:], el[:], mskt[:]))
            for tt in range(4):
                rw = v_inc(lambda v, tt=tt: v.tensor_scalar(w_f[:, tt * 8:(tt + 1) * 8],
                                                            wqt[:, tt * 8:(tt + 1) * 8],
                                                            dens[:, tt:tt + 1], None,
                                                            ALU.mult))
            wait("V", s_v, rw)
            v_wbf = v_inc(lambda v: v.tensor_copy(w_bf[:], w_f[:]))

            wait("T", s_v, v_wbf)
            for tt in range(4):
                t_run(lambda t, tt=tt: t.matmul(ps_mbc[0:8, tt * 128:(tt + 1) * 128],
                                                w_bf[:, tt * 8:(tt + 1) * 8], ident[:],
                                                start=True, stop=True))
            wT_done = t_inc(lambda t: t.matmul(ps_sq[0:1, 0:C], onescol[0:1, 0:1],
                                               row_a[0:1, 0:C], start=True, stop=True))
            wait("S", s_t, wT_done)
            v_wt = s_inc(lambda s: s.copy(wT_sb[:], ps_mbc[0:8, 0:C]))
            wait("T", s_s, v_wt)
            exp_done = t_inc(lambda t: t.matmul(ps_ibc[:, 0:C], expmat[:, :], wT_sb[:, 0:C],
                                                start=True, stop=True))
            wait("S", s_t, exp_done)
            s_wer = s_inc(lambda s: s.copy(wer_sb[:], ps_ibc[:, 0:C]))
            wait("V", s_s, s_wer)
            v_hp = v_inc(lambda v: v.tensor_mul(hp_bf[:], ps_h[:], wer_sb[:]))
            vv_h_used = v_hp
            vv_wer_used = v_hp
            vv_apply_bank_free = max(vv_apply_bank_free, v_hp)

            for oc in range(OC):
                sidx = ch * 64 + 32 + oc
                emit_wload(sidx)
                wait("T", s_w, 16 * (sidx + 1))
                if oc == 0:
                    wait("T", s_v, v_hp)
                if oc >= 2:
                    wait("T", s_s, ev[oc - 2])
                for k in range(KT):
                    t_run(lambda t, oc=oc, k=k: t.matmul(P[oc % 2][:],
                                                         wb[oc % 2][:, k * 128:(k + 1) * 128],
                                                         u[:, k * C:(k + 1) * C],
                                                         start=(k == 0), stop=False))
                t_run(lambda t, oc=oc: t.matmul(P[oc % 2][:],
                                                wb[oc % 2][:, KT * 128:(KT + 1) * 128],
                                                onestile[:], start=False, stop=False))
                tv_group_end[sidx] = t_inc(
                    lambda t, oc=oc: t.matmul(P[oc % 2][:], bTs[:, oc * 128:(oc + 1) * 128],
                                              hp_bf[:], start=False, stop=True))
                if oc >= 1:
                    emit_evac(oc - 1, ch * 64 + 32)
                    if (oc - 1) % 4 == 3:
                        emit_zstats_batch(oc - 1)
            emit_evac(OC - 1, ch * 64 + 32)
            emit_zstats_batch(OC - 1)
            stats_f_done = t_pair[OC - 1]

            va, sa = ln_rows(stats_f_done, with_zm=True)
            vv_stats_read = va
            wait("S", s_gc, u_gc_done)
            bc_f = bc_pair(row_zm, va, sa, vv_apply_bank_free, s_free_need=vv_u_done)
            wait("V", s_s, bc_f)
            wait("V", s_gc, u_gc_done)
            for k in range(KT):
                if k >= 2:
                    wait("V", s_o, 16 * (nout - 1))
                    wait("V", s_gc, f6k[k - 2])
                v_inc(lambda v, k=k: v.tensor_sub(yb[k % 2][:],
                                                  zb[:, k * C:(k + 1) * C], mbc_bf[:]))
                vc = v_inc(lambda v, k=k: v.tensor_mul(yb[k % 2][:], yb[k % 2][:],
                                                       ibc_bf[:]))
                wait("G", s_v, vc)
                f6k[k] = g_inc(lambda g, k=k: g.tensor_add(
                    ost[k % 2][:], yb[k % 2][:], xt[:, k * C:(k + 1) * C]))
                wait("Y", s_gc, f6k[k])
                def ostore(y, ch=ch, k=k):
                    return y.dma_start(outd[k * 128:(k + 1) * 128, ch * C:(ch + 1) * C],
                                       ost[k % 2][:]).then_inc(s_o, 16)
                emit("Y", ostore)
                nout += 1
                cnt["o"] += 16
            vv_f6_done = cnt["v"]
            sv_zb_free = cnt["v"]
            vv_apply_bank_free = cnt["v"]

        emit("Y", lambda y: y.wait_ge(s_o, cnt["o"]))

        with nc.Block() as blk:
            @blk.tensor
            def _(t):
                for f in ops["T"]:
                    f(t)

            @blk.vector
            def _(v):
                for f in ops["V"]:
                    f(v)

            @blk.scalar
            def _(s):
                for f in ops["S"]:
                    f(s)

            @blk.gpsimd
            def _(g):
                for f in ops["G"]:
                    f(g)

            @blk.sync
            def _(y):
                for f in ops["Y"]:
                    f(y)

    return nc


_NC_CACHE = None


def kernel(L0_attn_x, L0_attn_weight, L0_attn_bias, L0_ffn_W_base, L0_ffn_b_base,
           L0_ffn_W_router, L0_ffn_lora_A, L0_ffn_lora_B):
    global _NC_CACHE
    x = np.asarray(L0_attn_x, dtype=np.float32)
    W = np.ascontiguousarray(np.asarray(L0_attn_weight, dtype=np.float32))
    b_attn = np.ascontiguousarray(np.asarray(L0_attn_bias, dtype=np.float32))
    Wb = np.ascontiguousarray(np.asarray(L0_ffn_W_base, dtype=np.float32))
    b_base = np.ascontiguousarray(np.asarray(L0_ffn_b_base, dtype=np.float32))
    Wr = np.ascontiguousarray(np.asarray(L0_ffn_W_router, dtype=np.float32))
    lA = np.ascontiguousarray(np.asarray(L0_ffn_lora_A, dtype=np.float32))
    lB = np.ascontiguousarray(np.asarray(L0_ffn_lora_B, dtype=np.float32))

    B, S, _ = x.shape
    T = B * S
    xT_full = np.ascontiguousarray(x.reshape(T, D).T)

    def tile_w(Wm, bias):
        t = Wm.reshape(OC, 128, KT, 128).transpose(0, 3, 2, 1).reshape(OC, 128, D)
        ext = np.zeros((OC, 128, 128), np.float32)
        ext[:, 0, :] = bias.reshape(OC, 128)
        return np.ascontiguousarray(np.concatenate([t, ext], axis=2))
    wA_t = tile_w(W, b_attn)
    wF_t = tile_w(Wb, b_base)
    A2 = lA.reshape(128, D)
    aT_t = np.ascontiguousarray(A2.T.reshape(KT, 128, 128).transpose(1, 0, 2)
                                .reshape(128, D))
    B_all = np.ascontiguousarray(lB.transpose(1, 0, 2).reshape(D, 128))
    bT_t = np.ascontiguousarray(B_all.T)
    rT_t = np.ascontiguousarray(Wr.reshape(8, KT, 128).transpose(2, 1, 0)
                                .reshape(128, KT * 8))
    idm_h = np.eye(128, dtype=np.float32)
    exm_h = np.repeat(np.eye(8, dtype=np.float32), 16, axis=1).reshape(8, 128)

    if _NC_CACHE is None:
        _NC_CACHE = build_nc()
    nc = _NC_CACHE

    in_maps = []
    for core in range(NCORES):
        shard = np.ascontiguousarray(xT_full[:, core * TOK:(core + 1) * TOK])
        in_maps.append({
            "xT": shard, "wA": wA_t, "wF": wF_t, "aT": aT_t, "bT": bT_t,
            "rT": rT_t, "idm": idm_h, "exm": exm_h,
        })

    res = run_bass_kernel_spmd(nc, in_maps, core_ids=list(range(NCORES)))
    outT = np.concatenate([np.asarray(r["out"]) for r in res.results], axis=1)
    return np.ascontiguousarray(outT.T).reshape(B, S, D).astype(np.float32)


# version 27
# speedup vs baseline: 1.0826x; 1.0826x over previous
import numpy as np
from contextlib import ExitStack
import concourse.bass as bass
import concourse.mybir as mybir
from concourse.bass_utils import run_bass_kernel_spmd

F32 = mybir.dt.float32
BF16 = mybir.dt.bfloat16
AF = mybir.ActivationFunctionType
ALU = mybir.AluOpType
AX = mybir.AxisListType

D = 4096
TOK = 2048
C = 512
NCH = TOK // C
KT = D // 128
OC = D // 128
NCORES = 8
EPS = 1e-5
SCALING = 2.0


def build_nc():
    nc = bass.Bass("TRN2", target_bir_lowering=False)

    xT = nc.dram_tensor("xT", [D, TOK], F32, kind="ExternalInput").ap()
    wA = nc.dram_tensor("wA", [OC, 128, D + 128], F32, kind="ExternalInput").ap()
    wF = nc.dram_tensor("wF", [OC, 128, D + 128], F32, kind="ExternalInput").ap()
    aT = nc.dram_tensor("aT", [128, D], F32, kind="ExternalInput").ap()
    bT = nc.dram_tensor("bT", [128, D], F32, kind="ExternalInput").ap()
    rT = nc.dram_tensor("rT", [128, 256], F32, kind="ExternalInput").ap()
    idm = nc.dram_tensor("idm", [128, 128], F32, kind="ExternalInput").ap()
    exm = nc.dram_tensor("exm", [8, 128], F32, kind="ExternalInput").ap()
    outd = nc.dram_tensor("out", [D, TOK], F32, kind="ExternalOutput").ap()
    wsc = nc.dram_tensor("wsc", [2, OC, 128, D + 128], BF16, kind="Internal").ap()

    ops = {"T": [], "V": [], "S": [], "G": [], "Y": []}
    cnt = {"t": 0, "v": 0, "s": 0, "g": 0, "gc": 0,
           "x": [0, 0, 0, 0], "w": [0, 0], "o": [0, 0], "wsc": [0, 0]}

    def emit(e, fn):
        ops[e].append(fn)

    def wait(e, sem, val):
        if val > 0:
            emit(e, lambda eng, sem=sem, val=val: eng.wait_ge(sem, val))

    with ExitStack() as _es:
        xt = _es.enter_context(nc.sbuf_tensor("xt", [128, KT * C], F32))
        u = _es.enter_context(nc.sbuf_tensor("u", [128, KT * C], BF16))
        zb = _es.enter_context(nc.sbuf_tensor("zb", [128, OC * C], BF16))
        wb0 = _es.enter_context(nc.sbuf_tensor("wb0", [128, D + 128], BF16))
        wb1 = _es.enter_context(nc.sbuf_tensor("wb1", [128, D + 128], BF16))
        aTs = _es.enter_context(nc.sbuf_tensor("aTs", [128, D], BF16))
        bTs = _es.enter_context(nc.sbuf_tensor("bTs", [128, D], BF16))
        rTs = _es.enter_context(nc.sbuf_tensor("rTs", [128, 256], BF16))
        xbf0 = _es.enter_context(nc.sbuf_tensor("xbf0", [128, C], BF16))
        xbf1 = _es.enter_context(nc.sbuf_tensor("xbf1", [128, C], BF16))
        xbf2 = _es.enter_context(nc.sbuf_tensor("xbf2", [128, C], BF16))
        xbf3 = _es.enter_context(nc.sbuf_tensor("xbf3", [128, C], BF16))
        sq0 = _es.enter_context(nc.sbuf_tensor("sq0", [128, C], BF16))
        sq1 = _es.enter_context(nc.sbuf_tensor("sq1", [128, C], BF16))
        sq2 = _es.enter_context(nc.sbuf_tensor("sq2", [128, C], BF16))
        sq3 = _es.enter_context(nc.sbuf_tensor("sq3", [128, C], BF16))
        tbf = _es.enter_context(nc.sbuf_tensor("tbf", [128, C], BF16))
        ybf0 = _es.enter_context(nc.sbuf_tensor("ybf0", [128, C], BF16))
        ybf1 = _es.enter_context(nc.sbuf_tensor("ybf1", [128, C], BF16))
        mbc_bf = _es.enter_context(nc.sbuf_tensor("mbc_bf", [128, C], BF16))
        ibc_bf = _es.enter_context(nc.sbuf_tensor("ibc_bf", [128, C], BF16))
        ost0 = _es.enter_context(nc.sbuf_tensor("ost0", [128, C], F32))
        ost1 = _es.enter_context(nc.sbuf_tensor("ost1", [128, C], F32))
        row_m = _es.enter_context(nc.sbuf_tensor("row_m", [1, C], F32))
        row_inv = _es.enter_context(nc.sbuf_tensor("row_inv", [1, C], F32))
        row_zm = _es.enter_context(nc.sbuf_tensor("row_zm", [1, C], F32))
        row_a = _es.enter_context(nc.sbuf_tensor("row_a", [1, C], F32))
        row_b = _es.enter_context(nc.sbuf_tensor("row_b", [1, C], F32))
        ones128 = _es.enter_context(nc.sbuf_tensor("ones128", [128, 1], BF16))
        onescol = _es.enter_context(nc.sbuf_tensor("onescol", [1, 128], F32))
        onestile = _es.enter_context(nc.sbuf_tensor("onestile", [128, C], BF16))
        ident = _es.enter_context(nc.sbuf_tensor("ident", [128, 128], BF16))
        expmat = _es.enter_context(nc.sbuf_tensor("expmat", [8, 128], BF16))
        lg = _es.enter_context(nc.sbuf_tensor("lg", [128, 32], F32))
        el = _es.enter_context(nc.sbuf_tensor("el", [128, 32], F32))
        mskt = _es.enter_context(nc.sbuf_tensor("mskt", [128, 32], F32))
        l2t = _es.enter_context(nc.sbuf_tensor("l2t", [128, 8], F32))
        wqt = _es.enter_context(nc.sbuf_tensor("wqt", [128, 32], F32))
        w_f = _es.enter_context(nc.sbuf_tensor("w_f", [128, 32], F32))
        w_bf = _es.enter_context(nc.sbuf_tensor("w_bf", [128, 32], BF16))
        m1s = _es.enter_context(nc.sbuf_tensor("m1s", [128, 4], F32))
        nm1s = _es.enter_context(nc.sbuf_tensor("nm1s", [128, 4], F32))
        m2s = _es.enter_context(nc.sbuf_tensor("m2s", [128, 4], F32))
        e2s = _es.enter_context(nc.sbuf_tensor("e2s", [128, 4], F32))
        dens = _es.enter_context(nc.sbuf_tensor("dens", [128, 4], F32))
        wT_sb = _es.enter_context(nc.sbuf_tensor("wT_sb", [8, C], BF16))
        wer_sb = _es.enter_context(nc.sbuf_tensor("wer_sb", [128, C], BF16))
        hp_bf = _es.enter_context(nc.sbuf_tensor("hp_bf", [128, C], BF16))
        P0 = _es.enter_context(nc.psum_tensor("P0", [128, C], F32))
        P1 = _es.enter_context(nc.psum_tensor("P1", [128, C], F32))
        ps_sq = _es.enter_context(nc.psum_tensor("ps_sq", [1, C], F32))
        ps_sqq = _es.enter_context(nc.psum_tensor("ps_sqq", [1, C], F32))
        ps_mbc = _es.enter_context(nc.psum_tensor("ps_mbc", [128, C], F32))
        ps_ibc = _es.enter_context(nc.psum_tensor("ps_ibc", [128, C], F32))
        ps_h = _es.enter_context(nc.psum_tensor("ps_h", [128, C], F32))
        ps_mix = _es.enter_context(nc.psum_tensor("ps_mix", [128, C], F32))
        s_t = _es.enter_context(nc.semaphore("s_t"))
        s_v = _es.enter_context(nc.semaphore("s_v"))
        s_s = _es.enter_context(nc.semaphore("s_s"))
        s_x0 = _es.enter_context(nc.semaphore("s_x0"))
        s_x1 = _es.enter_context(nc.semaphore("s_x1"))
        s_x2 = _es.enter_context(nc.semaphore("s_x2"))
        s_x3 = _es.enter_context(nc.semaphore("s_x3"))
        s_w0 = _es.enter_context(nc.semaphore("s_w0"))
        s_w1 = _es.enter_context(nc.semaphore("s_w1"))
        s_o0 = _es.enter_context(nc.semaphore("s_o0"))
        s_o1 = _es.enter_context(nc.semaphore("s_o1"))
        s_wsc0 = _es.enter_context(nc.semaphore("s_wsc0"))
        s_wsc1 = _es.enter_context(nc.semaphore("s_wsc1"))
        s_g = _es.enter_context(nc.semaphore("s_g"))
        s_gc = _es.enter_context(nc.semaphore("s_gc"))
        P = [P0, P1]
        wb = [wb0, wb1]
        xbf = [xbf0, xbf1, xbf2, xbf3]
        sq = [sq0, sq1, sq2, sq3]
        ost = [ost0, ost1]
        wdram = [wA, wF]
        s_x = [s_x0, s_x1, s_x2, s_x3]
        s_w = [s_w0, s_w1]
        s_o = [s_o0, s_o1]
        s_wsc = [s_wsc0, s_wsc1]

        def t_inc(fn):
            cnt["t"] += 1
            emit("T", lambda t, fn=fn: fn(t).then_inc(s_t, 1))
            return cnt["t"]

        def t_run(fn):
            emit("T", fn)

        def v_inc(fn):
            cnt["v"] += 1
            emit("V", lambda v, fn=fn: fn(v).then_inc(s_v, 1))
            return cnt["v"]

        def s_inc(fn):
            cnt["s"] += 1
            emit("S", lambda s, fn=fn: fn(s).then_inc(s_s, 1))
            return cnt["s"]

        def g_inc(fn):
            cnt["gc"] += 1
            emit("G", lambda g, fn=fn: fn(g).then_inc(s_gc, 1))
            return cnt["gc"]

        emit("G", lambda g: g.dma_start(aTs[:], aT).then_inc(s_g, 16))
        emit("G", lambda g: g.dma_start(bTs[:], bT).then_inc(s_g, 16))
        emit("G", lambda g: g.dma_start(rTs[:], rT).then_inc(s_g, 16))
        cnt["g"] = 3 * 16
        emit("G", lambda g: g.dma_start(ident[:], idm).then_inc(s_g, 16))
        emit("G", lambda g: g.dma_start(expmat[:], exm).then_inc(s_g, 16))
        cnt["g"] += 2 * 16
        g_setup = cnt["g"]

        v_setup0 = v_inc(lambda v: v.memset(ones128[:], 1.0))
        v_setup1 = v_inc(lambda v: v.memset(onescol[:], 1.0))
        v_setup3 = v_inc(lambda v: v.memset(onestile[:], 1.0))
        v_setup = cnt["v"]

        tv_group_end = {}
        vv_f6_done = 0
        vv_u_done = 0
        vv_apply_bank_free = 0
        vv_stats_read = 0
        vv_lg_done = 0
        vv_wer_used = 0
        vv_h_used = 0
        sv_zb_free = 0

        nout = 0

        def ln_rows(ps_need_t, with_zm):
            tgt = row_zm if with_zm else row_m
            wait("V", s_t, ps_need_t)
            v_inc(lambda v, tgt=tgt: v.tensor_scalar_mul(tgt[:], ps_sq[0:1, :], 1.0 / D))
            v_inc(lambda v: v.tensor_scalar_mul(row_a[:], ps_sqq[0:1, :], 1.0 / D))
            v_inc(lambda v, tgt=tgt: v.tensor_mul(row_b[:], tgt[:], tgt[:]))
            v_inc(lambda v: v.tensor_sub(row_a[:], row_a[:], row_b[:]))
            va = v_inc(lambda v: v.tensor_scalar_add(row_a[:], row_a[:], EPS))
            wait("S", s_v, va)
            s_inc(lambda s: s.activation(row_b[:], row_a[:], AF.Ln))
            sa = s_inc(lambda s: s.activation(row_inv[:], row_b[:], AF.Exp, scale=-0.5))
            return cnt["v"], sa

        def bc_pair(m_row, v_need, s_need, t_free_need, s_free_need=0):
            wait("T", s_v, max(v_need, t_free_need))
            wait("T", s_s, s_need)
            ta = t_inc(lambda t, m_row=m_row: t.matmul(ps_mbc[:], onescol[:], m_row[:],
                                                       start=True, stop=True))
            tb = t_inc(lambda t: t.matmul(ps_ibc[:], onescol[:], row_inv[:],
                                          start=True, stop=True))
            wait("S", s_t, tb)
            if s_free_need:
                wait("S", s_v, s_free_need)
            s_inc(lambda s: s.copy(mbc_bf[:], ps_mbc[:]))
            sb = s_inc(lambda s: s.copy(ibc_bf[:], ps_ibc[:]))
            return sb

        f6k = {}
        for ch in range(NCH):
            t_pair = {}
            s_pair = {}
            for k in range(KT):
                if k % 8 == 0:
                    g = k // 8
                    if ch > 0:
                        wait("S", s_gc, f6k[8 * g + 7])
                    def xload(se, ch=ch, g=g):
                        src = xT[g * 1024:(g + 1) * 1024, ch * C:(ch + 1) * C] \
                            .rearrange("(k p) c -> p k c", p=128)
                        dst = xt[:, g * 8 * C:(g + 1) * 8 * C] \
                            .rearrange("p (k c) -> p k c", c=C)
                        return se.dma_start(dst, src).then_inc(s_x[g], 16)
                    emit("S", xload)
                    cnt["x"][g] += 16
                wait("S", s_x[k // 8], cnt["x"][k // 8])
                if k >= 4:
                    wait("S", s_t, t_pair[k - 4])
                s_inc(lambda s, k=k: s.copy(xbf[k % 4][:], xt[:, k * C:(k + 1) * C]))
                s_pair[k] = s_inc(lambda s, k=k: s.activation(sq[k % 4][:],
                                                              xt[:, k * C:(k + 1) * C],
                                                              AF.Square))
                if k % 2 == 1:
                    if k == 1:
                        wait("T", s_v, max(v_setup, vv_stats_read))
                    wait("T", s_s, s_pair[k])
                    for j in range(k - 1, k + 1):
                        t_run(lambda t, j=j: t.matmul(ps_sq[0:1, :], ones128[:, 0:1],
                                                      xbf[j % 4][:], start=(j == 0),
                                                      stop=(j == KT - 1)))
                    t_run(lambda t, k=k: t.matmul(ps_sqq[0:1, :], ones128[:, 0:1],
                                                  sq[(k - 1) % 4][:], start=(k == 1),
                                                  stop=False))
                    cap = t_inc(lambda t, k=k: t.matmul(ps_sqq[0:1, :], ones128[:, 0:1],
                                                        sq[k % 4][:], start=False,
                                                        stop=(k == KT - 1)))
                    t_pair[k - 1] = cap
                    t_pair[k] = cap
            stats_a_done = t_pair[KT - 1]

            va, sa = ln_rows(stats_a_done, with_zm=False)
            vv_stats_read = va
            bc_a = bc_pair(row_m, va, sa, vv_apply_bank_free, s_free_need=vv_f6_done)

            wait("V", s_s, bc_a)
            for g in range(4):
                wait("V", s_x[g], cnt["x"][g])
            uk = {}
            for k in range(KT):
                v_inc(lambda v, k=k: v.tensor_sub(tbf[:], xt[:, k * C:(k + 1) * C],
                                                  mbc_bf[:]))
                uk[k] = v_inc(lambda v, k=k: v.tensor_mul(u[:, k * C:(k + 1) * C],
                                                          tbf[:], ibc_bf[:]))
            vv_u_done = cnt["v"]
            vv_apply_bank_free = cnt["v"]

            def emit_wload(sidx):
                mat = (sidx % 64) // 32
                oc = sidx % 32
                if sidx >= 2:
                    wait("G", s_t, tv_group_end.get(sidx - 2, 0))
                par = sidx % 2
                if sidx < 64:
                    if sidx >= 2:
                        wait("G", s_wsc[par], cnt["wsc"][par])
                    emit("G", lambda g, mat=mat, oc=oc, par=par: g.dma_start(
                        wb[par][:], wdram[mat][oc]).then_inc(s_w[par], 16))
                    cnt["w"][par] += 16
                    wait("Y", s_w[par], cnt["w"][par])
                    emit("Y", lambda y, mat=mat, oc=oc, par=par: y.dma_start(
                        wsc[mat, oc], wb[par][:]).then_inc(s_wsc[par], 16))
                    cnt["wsc"][par] += 16
                else:
                    if sidx == 64:
                        wait("G", s_wsc[0], cnt["wsc"][0])
                        wait("G", s_wsc[1], cnt["wsc"][1])
                    emit("G", lambda g, mat=mat, oc=oc, par=par: g.dma_start(
                        wb[par][:], wsc[mat, oc]).then_inc(s_w[par], 16))
                    cnt["w"][par] += 16

            def emit_zstats_batch(po_hi):
                wait("T", s_s, ev[po_hi])
                if po_hi == 3:
                    wait("T", s_v, vv_stats_read)
                for j in range(po_hi - 3, po_hi + 1):
                    t_run(lambda t, j=j: t.matmul(ps_sq[0:1, :], ones128[:, 0:1],
                                                  zb[:, j * C:(j + 1) * C],
                                                  start=(j == 0), stop=(j == OC - 1)))
                for j in range(po_hi - 3, po_hi):
                    t_run(lambda t, j=j: t.matmul(ps_sqq[0:1, :], ones128[:, 0:1],
                                                  sq[j % 4][:], start=(j == 0), stop=False))
                cap = t_inc(lambda t, po_hi=po_hi: t.matmul(ps_sqq[0:1, :], ones128[:, 0:1],
                                                            sq[po_hi % 4][:], start=False,
                                                            stop=(po_hi == OC - 1)))
                for j in range(po_hi - 3, po_hi + 1):
                    t_pair[j] = cap

            def emit_evac(po, base):
                wait("S", s_t, tv_group_end[base + po])
                if po >= 4:
                    wait("S", s_t, t_pair[po - 4])
                if base % 64 == 0 and ch > 0 and po == 0:
                    wait("S", s_v, sv_zb_free)
                s_inc(lambda s, po=po: s.copy(zb[:, po * C:(po + 1) * C], P[po % 2][:]))
                ev[po] = s_inc(lambda s, po=po: s.activation(sq[po % 4][:], P[po % 2][:],
                                                             AF.Square))

            ev = {}
            for oc in range(OC):
                sidx = ch * 64 + oc
                emit_wload(sidx)
                wait("T", s_w[sidx % 2], 16 * (sidx // 2 + 1))
                if oc == 0:
                    wait("T", s_g, g_setup)
                if oc >= 2:
                    wait("T", s_s, ev[oc - 2])
                for k in range(KT):
                    if oc == 0:
                        wait("T", s_v, uk[k])
                    t_run(lambda t, oc=oc, k=k: t.matmul(P[oc % 2][:],
                                                         wb[oc % 2][:, k * 128:(k + 1) * 128],
                                                         u[:, k * C:(k + 1) * C],
                                                         start=(k == 0), stop=False))
                tv_group_end[sidx] = t_inc(
                    lambda t, oc=oc: t.matmul(P[oc % 2][:],
                                              wb[oc % 2][:, KT * 128:(KT + 1) * 128],
                                              onestile[:], start=False, stop=True))
                if oc >= 1:
                    emit_evac(oc - 1, ch * 64)
                    if (oc - 1) % 4 == 3:
                        emit_zstats_batch(oc - 1)
            emit_evac(OC - 1, ch * 64)
            emit_zstats_batch(OC - 1)
            stats_b_done = t_pair[OC - 1]

            va, sa = ln_rows(stats_b_done, with_zm=True)
            vv_stats_read = va
            bc_c = bc_pair(row_zm, va, sa, vv_apply_bank_free, s_free_need=vv_u_done)
            wait("V", s_s, bc_c)
            wait("V", s_o[0], cnt["o"][0])
            wait("V", s_o[1], cnt["o"][1])
            c3k = {}
            yb = [ybf0, ybf1]
            for k in range(KT):
                if k >= 2:
                    wait("V", s_gc, c3k[k - 2])
                v_inc(lambda v, k=k: v.tensor_sub(yb[k % 2][:],
                                                  zb[:, k * C:(k + 1) * C], mbc_bf[:]))
                vc = v_inc(lambda v, k=k: v.tensor_mul(yb[k % 2][:], yb[k % 2][:],
                                                       ibc_bf[:]))
                wait("G", s_v, vc)
                c3k[k] = g_inc(lambda g, k=k: g.tensor_add(
                    xt[:, k * C:(k + 1) * C], xt[:, k * C:(k + 1) * C], yb[k % 2][:]))
            vv_c_done = cnt["v"]
            vv_apply_bank_free = cnt["v"]

            for k in range(KT):
                wait("S", s_gc, c3k[k])
                if k >= 4:
                    wait("S", s_t, t_pair[k - 4])
                s_pair[k] = s_inc(lambda s, k=k: s.activation(sq[k % 4][:],
                                                              xt[:, k * C:(k + 1) * C],
                                                              AF.Square))
                if k % 2 == 1:
                    if k == 1:
                        wait("T", s_v, vv_stats_read)
                    wait("T", s_s, s_pair[k])
                    t_run(lambda t, k=k: t.matmul(ps_sqq[0:1, :], ones128[:, 0:1],
                                                  sq[(k - 1) % 4][:], start=(k == 1),
                                                  stop=False))
                    cap = t_inc(lambda t, k=k: t.matmul(ps_sqq[0:1, :], ones128[:, 0:1],
                                                        sq[k % 4][:], start=False,
                                                        stop=(k == KT - 1)))
                    t_pair[k - 1] = cap
                    t_pair[k] = cap
            stats_d_done = t_pair[KT - 1]

            wait("V", s_t, stats_d_done)
            v_inc(lambda v: v.tensor_scalar_mul(row_a[:], ps_sqq[0:1, :], 1.0 / D))
            v_inc(lambda v: v.tensor_mul(row_b[:], row_m[:], row_m[:]))
            v_inc(lambda v: v.tensor_sub(row_a[:], row_a[:], row_b[:]))
            va = v_inc(lambda v: v.tensor_scalar_add(row_a[:], row_a[:], EPS))
            vv_stats_read = va
            wait("S", s_v, va)
            s_inc(lambda s: s.activation(row_b[:], row_a[:], AF.Ln))
            sa = s_inc(lambda s: s.activation(row_inv[:], row_b[:], AF.Exp, scale=-0.5))
            bc_d = bc_pair(row_m, va, sa, vv_apply_bank_free, s_free_need=vv_c_done)

            wait("V", s_s, bc_d)
            wait("V", s_gc, c3k[KT - 1])
            for k in range(KT):
                v_inc(lambda v, k=k: v.tensor_sub(tbf[:], xt[:, k * C:(k + 1) * C],
                                                  mbc_bf[:]))
                uk[k] = v_inc(lambda v, k=k: v.tensor_mul(u[:, k * C:(k + 1) * C],
                                                          tbf[:], ibc_bf[:]))
            vv_u_done = cnt["v"]
            vv_apply_bank_free = cnt["v"]

            wait("T", s_v, vv_wer_used)
            for tt in range(4):
                for k in range(KT):
                    if tt == 0:
                        wait("T", s_v, uk[k])
                    t_run(lambda t, tt=tt, k=k: t.matmul(
                        ps_mix[:, tt * 8:(tt + 1) * 8],
                        u[:, k * C + tt * 128:k * C + (tt + 1) * 128],
                        rTs[:, k * 8:(k + 1) * 8],
                        start=(k == 0), stop=(k == KT - 1)))
            wait("T", s_v, vv_h_used)
            logits_done = t_inc(lambda t: t.matmul(ps_h[:], aTs[:, 0:128], u[:, 0:C],
                                                   start=True, stop=False))
            for k in range(1, KT):
                t_run(lambda t, k=k: t.matmul(ps_h[:], aTs[:, k * 128:(k + 1) * 128],
                                              u[:, k * C:(k + 1) * C],
                                              start=False, stop=(k == KT - 1)))

            wait("V", s_t, logits_done)
            vv_lg_done = v_inc(lambda v: v.tensor_copy(lg[:], ps_mix[:, 0:32]))
            for tt in range(4):
                r1 = v_inc(lambda v, tt=tt: v.reduce_max(m1s[:, tt:tt + 1],
                                                         lg[:, tt * 8:(tt + 1) * 8],
                                                         axis=AX.X))
                wait("V", s_v, r1)
                v_inc(lambda v, tt=tt: v.tensor_scalar_mul(nm1s[:, tt:tt + 1],
                                                           m1s[:, tt:tt + 1], -1.0))
                rq = v_inc(lambda v, tt=tt: v.tensor_scalar(mskt[:, tt * 8:(tt + 1) * 8],
                                                            lg[:, tt * 8:(tt + 1) * 8],
                                                            m1s[:, tt:tt + 1], None,
                                                            ALU.is_ge))
                wait("V", s_v, rq)
                rl = v_inc(lambda v, tt=tt: v.scalar_tensor_tensor(l2t[:],
                                                                   mskt[:, tt * 8:(tt + 1) * 8],
                                                                   -1e30,
                                                                   lg[:, tt * 8:(tt + 1) * 8],
                                                                   ALU.mult, ALU.add))
                wait("V", s_v, rl)
                r2 = v_inc(lambda v, tt=tt: v.reduce_max(m2s[:, tt:tt + 1], l2t[:],
                                                         axis=AX.X))
                wait("V", s_v, r2)
                v_inc(lambda v, tt=tt: v.tensor_scalar(mskt[:, tt * 8:(tt + 1) * 8],
                                                       lg[:, tt * 8:(tt + 1) * 8],
                                                       m2s[:, tt:tt + 1], None, ALU.is_ge))
            v_rt1 = cnt["v"]
            wait("S", s_v, v_rt1)
            for tt in range(4):
                s_inc(lambda s, tt=tt: s.activation(el[:, tt * 8:(tt + 1) * 8],
                                                    lg[:, tt * 8:(tt + 1) * 8],
                                                    AF.Exp, bias=nm1s[:, tt:tt + 1]))
                s_inc(lambda s, tt=tt: s.activation(e2s[:, tt:tt + 1], m2s[:, tt:tt + 1],
                                                    AF.Exp, bias=nm1s[:, tt:tt + 1]))
            s_rt = cnt["s"]
            wait("V", s_s, s_rt)
            ra = v_inc(lambda v: v.tensor_scalar_add(dens[:], e2s[:], 1.0))
            wait("V", s_v, ra)
            rb = v_inc(lambda v: v.reciprocal(dens[:], dens[:]))
            wait("V", s_v, rb)
            rc = v_inc(lambda v: v.tensor_scalar_mul(dens[:], dens[:], SCALING))
            wait("V", s_v, rc)
            v_inc(lambda v: v.tensor_mul(wqt[:], el[:], mskt[:]))
            for tt in range(4):
                rw = v_inc(lambda v, tt=tt: v.tensor_scalar(w_f[:, tt * 8:(tt + 1) * 8],
                                                            wqt[:, tt * 8:(tt + 1) * 8],
                                                            dens[:, tt:tt + 1], None,
                                                            ALU.mult))
            wait("V", s_v, rw)
            v_wbf = v_inc(lambda v: v.tensor_copy(w_bf[:], w_f[:]))

            wait("T", s_v, v_wbf)
            for tt in range(4):
                t_run(lambda t, tt=tt: t.matmul(ps_mbc[0:8, tt * 128:(tt + 1) * 128],
                                                w_bf[:, tt * 8:(tt + 1) * 8], ident[:],
                                                start=True, stop=True))
            wT_done = t_inc(lambda t: t.matmul(ps_sq[0:1, 0:C], onescol[0:1, 0:1],
                                               row_a[0:1, 0:C], start=True, stop=True))
            wait("S", s_t, wT_done)
            v_wt = s_inc(lambda s: s.copy(wT_sb[:], ps_mbc[0:8, 0:C]))
            wait("T", s_s, v_wt)
            exp_done = t_inc(lambda t: t.matmul(ps_ibc[:, 0:C], expmat[:, :], wT_sb[:, 0:C],
                                                start=True, stop=True))
            wait("S", s_t, exp_done)
            s_wer = s_inc(lambda s: s.copy(wer_sb[:], ps_ibc[:, 0:C]))
            wait("V", s_s, s_wer)
            v_hp = v_inc(lambda v: v.tensor_mul(hp_bf[:], ps_h[:], wer_sb[:]))
            vv_h_used = v_hp
            vv_wer_used = v_hp
            vv_apply_bank_free = max(vv_apply_bank_free, v_hp)

            for oc in range(OC):
                sidx = ch * 64 + 32 + oc
                emit_wload(sidx)
                wait("T", s_w[sidx % 2], 16 * (sidx // 2 + 1))
                if oc == 0:
                    wait("T", s_v, v_hp)
                if oc >= 2:
                    wait("T", s_s, ev[oc - 2])
                for k in range(KT):
                    t_run(lambda t, oc=oc, k=k: t.matmul(P[oc % 2][:],
                                                         wb[oc % 2][:, k * 128:(k + 1) * 128],
                                                         u[:, k * C:(k + 1) * C],
                                                         start=(k == 0), stop=False))
                t_run(lambda t, oc=oc: t.matmul(P[oc % 2][:],
                                                wb[oc % 2][:, KT * 128:(KT + 1) * 128],
                                                onestile[:], start=False, stop=False))
                tv_group_end[sidx] = t_inc(
                    lambda t, oc=oc: t.matmul(P[oc % 2][:], bTs[:, oc * 128:(oc + 1) * 128],
                                              hp_bf[:], start=False, stop=True))
                if oc >= 1:
                    emit_evac(oc - 1, ch * 64 + 32)
                    if (oc - 1) % 4 == 3:
                        emit_zstats_batch(oc - 1)
            emit_evac(OC - 1, ch * 64 + 32)
            emit_zstats_batch(OC - 1)
            stats_f_done = t_pair[OC - 1]

            va, sa = ln_rows(stats_f_done, with_zm=True)
            vv_stats_read = va
            bc_f = bc_pair(row_zm, va, sa, vv_apply_bank_free, s_free_need=vv_u_done)
            wait("V", s_s, bc_f)
            for k in range(KT):
                if k >= 2:
                    wait("V", s_o[k % 2], cnt["o"][k % 2])
                    wait("V", s_gc, f6k[k - 2])
                v_inc(lambda v, k=k: v.tensor_sub(yb[k % 2][:],
                                                  zb[:, k * C:(k + 1) * C], mbc_bf[:]))
                vc = v_inc(lambda v, k=k: v.tensor_mul(yb[k % 2][:], yb[k % 2][:],
                                                       ibc_bf[:]))
                wait("G", s_v, vc)
                f6k[k] = g_inc(lambda g, k=k: g.tensor_add(
                    ost[k % 2][:], yb[k % 2][:], xt[:, k * C:(k + 1) * C]))
                wait("Y", s_gc, f6k[k])
                def ostore(y, ch=ch, k=k):
                    return y.dma_start(outd[k * 128:(k + 1) * 128, ch * C:(ch + 1) * C],
                                       ost[k % 2][:]).then_inc(s_o[k % 2], 16)
                emit("Y", ostore)
                nout += 1
                cnt["o"][k % 2] += 16
            vv_f6_done = cnt["v"]
            sv_zb_free = cnt["v"]
            vv_apply_bank_free = cnt["v"]

        emit("Y", lambda y: y.wait_ge(s_o0, cnt["o"][0]))
        emit("Y", lambda y: y.wait_ge(s_o1, cnt["o"][1]))

        with nc.Block() as blk:
            @blk.tensor
            def _(t):
                for f in ops["T"]:
                    f(t)

            @blk.vector
            def _(v):
                for f in ops["V"]:
                    f(v)

            @blk.scalar
            def _(s):
                for f in ops["S"]:
                    f(s)

            @blk.gpsimd
            def _(g):
                for f in ops["G"]:
                    f(g)

            @blk.sync
            def _(y):
                for f in ops["Y"]:
                    f(y)

    return nc


_NC_CACHE = None


def kernel(L0_attn_x, L0_attn_weight, L0_attn_bias, L0_ffn_W_base, L0_ffn_b_base,
           L0_ffn_W_router, L0_ffn_lora_A, L0_ffn_lora_B):
    global _NC_CACHE
    x = np.asarray(L0_attn_x, dtype=np.float32)
    W = np.ascontiguousarray(np.asarray(L0_attn_weight, dtype=np.float32))
    b_attn = np.ascontiguousarray(np.asarray(L0_attn_bias, dtype=np.float32))
    Wb = np.ascontiguousarray(np.asarray(L0_ffn_W_base, dtype=np.float32))
    b_base = np.ascontiguousarray(np.asarray(L0_ffn_b_base, dtype=np.float32))
    Wr = np.ascontiguousarray(np.asarray(L0_ffn_W_router, dtype=np.float32))
    lA = np.ascontiguousarray(np.asarray(L0_ffn_lora_A, dtype=np.float32))
    lB = np.ascontiguousarray(np.asarray(L0_ffn_lora_B, dtype=np.float32))

    B, S, _ = x.shape
    T = B * S
    xT_full = np.ascontiguousarray(x.reshape(T, D).T)

    def tile_w(Wm, bias):
        t = Wm.reshape(OC, 128, KT, 128).transpose(0, 3, 2, 1).reshape(OC, 128, D)
        ext = np.zeros((OC, 128, 128), np.float32)
        ext[:, 0, :] = bias.reshape(OC, 128)
        return np.ascontiguousarray(np.concatenate([t, ext], axis=2))
    wA_t = tile_w(W, b_attn)
    wF_t = tile_w(Wb, b_base)
    A2 = lA.reshape(128, D)
    aT_t = np.ascontiguousarray(A2.T.reshape(KT, 128, 128).transpose(1, 0, 2)
                                .reshape(128, D))
    B_all = np.ascontiguousarray(lB.transpose(1, 0, 2).reshape(D, 128))
    bT_t = np.ascontiguousarray(B_all.T)
    rT_t = np.ascontiguousarray(Wr.reshape(8, KT, 128).transpose(2, 1, 0)
                                .reshape(128, KT * 8))
    idm_h = np.eye(128, dtype=np.float32)
    exm_h = np.repeat(np.eye(8, dtype=np.float32), 16, axis=1).reshape(8, 128)

    if _NC_CACHE is None:
        _NC_CACHE = build_nc()
    nc = _NC_CACHE

    in_maps = []
    for core in range(NCORES):
        shard = np.ascontiguousarray(xT_full[:, core * TOK:(core + 1) * TOK])
        in_maps.append({
            "xT": shard, "wA": wA_t, "wF": wF_t, "aT": aT_t, "bT": bT_t,
            "rT": rT_t, "idm": idm_h, "exm": exm_h,
        })

    res = run_bass_kernel_spmd(nc, in_maps, core_ids=list(range(NCORES)))
    outT = np.concatenate([np.asarray(r["out"]) for r in res.results], axis=1)
    return np.ascontiguousarray(outT.T).reshape(B, S, D).astype(np.float32)
